# version 02 (final)
# baseline (speedup 1.0000x reference)
import numpy as np
import ml_dtypes
from contextlib import ExitStack

import concourse.bass as bass
import concourse.mybir as mybir
from concourse.bass_utils import run_bass_kernel_spmd

F32 = mybir.dt.float32
F16 = mybir.dt.float16
BF16 = mybir.dt.bfloat16
I32 = mybir.dt.int32
A = mybir.AluOpType
AF = mybir.ActivationFunctionType

DM = 4096
NIC = DM // 128
NH, HD = 32, 128
SEQ = 4096
TQ = 1024
NT = TQ // 128
IFF = 11008
NJB = IFF // 128
KTOP = 819
RS128 = float(1.0 / np.sqrt(128.0))
NEGBIG = -30000.0

JBIS = 9
WLO, WHI = 0.70, 0.98
DBIS = float((WHI - WLO) * (2.0 ** -JBIS))
CENG = ["se", "dve", "se", "dve", "se", "dve", "se", "dve", "se"][:JBIS]

ENGS = ("sync", "vector", "scalar", "tensor", "gpsimd")


class Em:

    def __init__(self, nc, S):
        self.nc = nc
        self.S = S
        self.plan = {e: [] for e in ENGS}
        self.sem = {}
        for name in ("v", "s", "t", "g"):
            self.sem[name] = S(nc.semaphore("sem_" + name))
        self.cnt = {k: 0 for k in self.sem}
        self.writer = {}
        self.readers = {}
        self.hw = {e: {} for e in ENGS}
        self.esem = {"vector": ("v", 1), "scalar": ("s", 1),
                     "tensor": ("t", 1), "gpsimd": ("g", 1)}

    def _dma_sem(self, key):
        name = "d_" + key
        if name not in self.sem:
            self.sem[name] = self.S(self.nc.semaphore(name))
            self.cnt[name] = 0
        return name

    def _waits(self, eng, sname, reads, writes):
        need = {}
        for k in reads:
            w = self.writer.get(k)
            if w:
                s, v = w
                if not (eng == "tensor" and s == "t"):
                    need[s] = max(need.get(s, 0), v)
        for k in writes:
            w = self.writer.get(k)
            if w and w[0] != sname:
                need[w[0]] = max(need.get(w[0], 0), w[1])
            for s, v in self.readers.get(k, {}).items():
                if s != sname:
                    need[s] = max(need.get(s, 0), v)
        out = []
        hw = self.hw[eng]
        for s, v in need.items():
            if hw.get(s, 0) < v:
                out.append((s, v))
                hw[s] = v
        return out

    def op(self, eng, build, reads=(), writes=(), dma=False):
        if dma:
            assert len(writes) == 1, f"dma op must have exactly 1 write key: {writes}"
            sname, inc = self._dma_sem(writes[0]), 16
        else:
            sname, inc = self.esem[eng]
        waits = self._waits(eng, sname, reads, writes)
        self.cnt[sname] += inc
        val = self.cnt[sname]
        for k in writes:
            self.writer[k] = (sname, val)
            self.readers[k] = {}
        for k in reads:
            self.readers.setdefault(k, {})[sname] = val
        sem = self.sem[sname]
        sems = self.sem

        def fn(q, waits=waits, build=build, sem=sem, inc=inc):
            for s, v in waits:
                q.wait_ge(sems[s], v)
            build(q).then_inc(sem, inc)
        self.plan[eng].append(fn)

    def barrier(self):
        tgt = {s: v for s, v in self.cnt.items() if v > 0}
        sems = self.sem
        for e in ENGS:
            hw = self.hw[e]
            todo = tuple((s, v) for s, v in tgt.items() if hw.get(s, 0) < v)
            for s, v in todo:
                hw[s] = v

            def fn(q, todo=todo):
                for s, v in todo:
                    q.wait_ge(sems[s], v)
            self.plan[e].append(fn)


def build_program():
    nc = bass.Bass("TRN2", target_bir_lowering=False)

    x_h = nc.dram_tensor("x", [TQ, DM], F32, kind="ExternalInput").ap()
    wqT_h = nc.dram_tensor("wqT", [DM, DM], F32, kind="ExternalInput").ap()
    wkT_h = nc.dram_tensor("wkT", [DM, DM], F32, kind="ExternalInput").ap()
    wvT_h = nc.dram_tensor("wvT", [DM, DM], F32, kind="ExternalInput").ap()
    woT_h = nc.dram_tensor("woT", [DM, DM], F32, kind="ExternalInput").ap()
    bq_h = nc.dram_tensor("bq", [128, NH], F32, kind="ExternalInput").ap()
    bk_h = nc.dram_tensor("bk", [128, NH], F32, kind="ExternalInput").ap()
    bv_h = nc.dram_tensor("bv", [128, NH], F32, kind="ExternalInput").ap()
    bo_h = nc.dram_tensor("bo", [128, NIC], F32, kind="ExternalInput").ap()
    qg_h = nc.dram_tensor("qg", [IFF, DM], I32, kind="ExternalInput").ap()
    qu_h = nc.dram_tensor("qu", [IFF, DM], I32, kind="ExternalInput").ap()
    qd_h = nc.dram_tensor("qd", [DM, IFF], I32, kind="ExternalInput").ap()
    sg_h = nc.dram_tensor("sg", [IFF, 32], F32, kind="ExternalInput").ap()
    zg_h = nc.dram_tensor("zg", [IFF, 32], F32, kind="ExternalInput").ap()
    su_h = nc.dram_tensor("su", [IFF, 32], F32, kind="ExternalInput").ap()
    zu_h = nc.dram_tensor("zu", [IFF, 32], F32, kind="ExternalInput").ap()
    sd_h = nc.dram_tensor("sd", [DM, NJB], F32, kind="ExternalInput").ap()
    zd_h = nc.dram_tensor("zd", [DM, NJB], F32, kind="ExternalInput").ap()
    out_h = nc.dram_tensor("out", [TQ, DM], F32, kind="ExternalOutput").ap()

    kcc_in = nc.dram_tensor("kcc_in", [NH * HD, TQ], BF16)
    kcc_out = nc.dram_tensor("kcc_out", [4 * NH * HD, TQ], BF16)
    vcc_in = nc.dram_tensor("vcc_in", [NH * TQ, HD], BF16)
    vcc_out = nc.dram_tensor("vcc_out", [4 * NH * TQ, HD], BF16)
    hT_d = nc.dram_tensor("hT_d", [IFF, TQ], BF16)
    wdt_d = nc.dram_tensor("wdt_d", [NJB * DM, 128], BF16)
    ffn_d = nc.dram_tensor("ffn_d", [DM, TQ], F32)
    ctx_d = nc.dram_tensor("ctx_d", [DM, TQ], BF16)

    es = ExitStack()
    S = es.enter_context
    em = Em(nc, S)

    xnT = S(nc.sbuf_tensor("xnT", [128, NIC * TQ], BF16))
    AR = S(nc.sbuf_tensor("AR", [128, 55296], BF16))
    SM = S(nc.sbuf_tensor("SM", [128, 512], F32))

    def arb(c0, n):
        return AR[:, c0:c0 + n]

    def arf32(c0, n):
        return AR[:].bitcast(F32)[:, c0 // 2: c0 // 2 + n]

    def arf16(c0, n):
        return AR[:].bitcast(F16)[:, c0:c0 + n]

    def ari32(c0, n):
        return AR[:].bitcast(I32)[:, c0 // 2: c0 // 2 + n]

    PB = [S(nc.psum_tensor(f"PB{i}", [128, 512], F32)) for i in range(6)]
    PT = [S(nc.psum_tensor(f"PT{i}", [128, 1024], BF16)) for i in range(2)]

    idn = S(nc.sbuf_tensor("idn", [128, 128], BF16))
    idf = S(nc.sbuf_tensor("idf", [128, 128], F32))
    iot = S(nc.sbuf_tensor("iot", [128, 128], BF16))
    io8 = S(nc.sbuf_tensor("io8", [128, 8], F32))
    onesc = S(nc.sbuf_tensor("onesc", [128, 1], BF16))
    kb16 = S(nc.sbuf_tensor("kb16", [128, 1], BF16))
    lamsq = S(nc.sbuf_tensor("lamsq", [128, 1], F32))
    lamb = S(nc.sbuf_tensor("lamb", [128, 1], BF16))
    bqs = S(nc.sbuf_tensor("bqs", [128, NH], F32))
    bks = S(nc.sbuf_tensor("bks", [128, NH], F32))
    bvs = S(nc.sbuf_tensor("bvs", [128, NH], F32))
    bosc = S(nc.sbuf_tensor("bosc", [128, NIC], F32))
    ZS = S(nc.sbuf_tensor("ZS", [128, 128], F32))
    ZSD = S(nc.sbuf_tensor("ZSD", [128, 4 * NJB], F32))
    m8 = S(nc.sbuf_tensor("m8", [128, 16], F16))
    t8 = S(nc.sbuf_tensor("t8", [128, 16], BF16))
    j8 = S(nc.sbuf_tensor("j8", [128, 16], BF16))
    s8 = S(nc.sbuf_tensor("s8", [128, 16], F32))
    mlt = S(nc.sbuf_tensor("mlt", [128, 2], mybir.dt.uint8))
    dead = S(nc.sbuf_tensor("dead", [128, SEQ], BF16))
    qwc = S(nc.sbuf_tensor("qwc", [128, 2 * 128], F32))
    qwb = S(nc.sbuf_tensor("qwb", [128, 2 * 128], BF16))
    sqb = S(nc.sbuf_tensor("sqb", [128, TQ], BF16))
    prm = S(nc.sbuf_tensor("prm", [128, 3 * NT], F32))
    ctst = S(nc.sbuf_tensor("ctst", [128, 256], BF16))
    dqc = S(nc.sbuf_tensor("dqc", [128, 2 * 128], I32))
    dqb = S(nc.sbuf_tensor("dqb", [128, 2 * 128], BF16))
    dlh = S(nc.sbuf_tensor("dlh", [128, 2 * 128], BF16))
    dht = S(nc.sbuf_tensor("dht", [128, 2 * 512], BF16))
    dos = S(nc.sbuf_tensor("dos", [128, 2 * 512], F32))

    V, C, T, G, Y = "vector", "scalar", "tensor", "gpsimd", "sync"

    em.op(G, lambda q: q.iota(io8[:], [[1, 8]], channel_multiplier=0,
                              allow_small_or_imprecise_dtypes=True), writes=("io8",))
    em.op(G, lambda q: q.iota(iot[:], [[1, 128]], channel_multiplier=-1,
                              allow_small_or_imprecise_dtypes=True), writes=("iot",))
    em.op(V, lambda q: q.tensor_scalar(out=idn[:], in0=iot[:], scalar1=0.0,
                                       scalar2=None, op0=A.is_equal),
          reads=("iot",), writes=("idn",))
    em.op(V, lambda q: q.tensor_copy(idf[:], idn[:]), reads=("idn",), writes=("idf",))
    em.op(V, lambda q: q.memset(onesc[:], 1.0), writes=("onesc",))
    em.op(Y, lambda q: q.dma_start(out=bqs[:], in_=bq_h[:]), writes=("bqs",), dma=True)
    em.op(Y, lambda q: q.dma_start(out=bks[:], in_=bk_h[:]), writes=("bks",), dma=True)
    em.op(Y, lambda q: q.dma_start(out=bvs[:], in_=bv_h[:]), writes=("bvs",), dma=True)
    em.op(Y, lambda q: q.dma_start(out=bosc[:], in_=bo_h[:]), writes=("bosc",), dma=True)

    XB = [0, 8192]
    XN = [16384, 20480]
    for tt in range(NT):
        sl = tt % 2
        xb, xn_c = XB[sl], XN[sl]
        em.op(Y, lambda q, tt=tt, xb=xb: q.dma_start(
            out=arf32(xb, 4096), in_=x_h[tt * 128:(tt + 1) * 128, :]),
            writes=(f"xb{sl}",), dma=True)
        for bs in range(8):
            em.op(V, lambda q, xb=xb, bs=bs: q.bn_stats(
                SM[:, bs * 6:(bs + 1) * 6], arf32(xb + bs * 1024, 512)),
                reads=(f"xb{sl}",), writes=("bnst",))
        em.op(V, lambda q: q.bn_aggr(SM[:, 48:50], SM[:, 0:48]),
              reads=("bnst",), writes=("bnag",))
        em.op(V, lambda q: q.tensor_scalar(out=SM[:, 50:51], in0=SM[:, 49:50],
                                           scalar1=1e-5, scalar2=None, op0=A.add),
              reads=("bnag",), writes=("vare",))
        em.op(C, lambda q: q.activation(out=SM[:, 51:52], in_=SM[:, 50:51], func=AF.Ln),
              reads=("vare",), writes=("lnv",))
        em.op(C, lambda q: q.activation(out=SM[:, 52:53], in_=SM[:, 51:52],
                                        func=AF.Exp, scale=-0.5),
              reads=("lnv",), writes=("rstd",))
        em.op(V, lambda q, xb=xb, xn_c=xn_c: q.tensor_scalar(
            out=arb(xn_c, 4096), in0=arf32(xb, 4096),
            scalar1=SM[:, 48:49], scalar2=SM[:, 52:53],
            op0=A.subtract, op1=A.mult),
            reads=(f"xb{sl}", "bnag", "rstd"), writes=(f"xn{sl}",))
        for bk_i in range(4):
            pt = bk_i % 2
            for c in range(8):
                ic = bk_i * 8 + c
                em.op(T, lambda q, xn_c=xn_c, ic=ic, c=c, pt=pt: q.transpose(
                    PT[pt][:, c * 128:(c + 1) * 128],
                    arb(xn_c + ic * 128, 128), idn[:]),
                    reads=(f"xn{sl}", "idn"), writes=(f"PT{pt}",))
            em.op(V, lambda q, pt=pt, bk_i=bk_i, tt=tt: q.tensor_copy(
                xnT[:].rearrange("p (c t) -> p c t", t=TQ)[
                    :, bk_i * 8:(bk_i + 1) * 8, tt * 128:(tt + 1) * 128],
                PT[pt][:].rearrange("p (c u) -> p c u", u=128)),
                reads=(f"PT{pt}",), writes=("xnT",))

    em.barrier()

    WSL = [0, 8192]
    WCC = [16384, 20480]
    KEV = [24576, 25600]
    VTE = 26624

    def head_weight(mat_h, h):
        sl, cc = WSL[h % 2], WCC[h % 2]
        em.op(Y, lambda q, h=h, sl=sl, mat_h=mat_h: q.dma_start(
            out=arf32(sl, 4096),
            in_=mat_h[:, h * HD:(h + 1) * HD].rearrange("(c p) d -> p c d", p=128)),
            writes=(f"wsl{h % 2}",), dma=True)
        em.op(V, lambda q, sl=sl, cc=cc: q.tensor_copy(arb(cc, 4096), arf32(sl, 4096)),
              reads=(f"wsl{h % 2}",), writes=(f"wcc{h % 2}",))
        return cc

    def head_proj(cc, h):
        for th2 in range(2):
            for ic in range(NIC):
                em.op(T, lambda q, cc=cc, ic=ic, th2=th2: q.matmul(
                    PB[th2][:], arb(cc + ic * 128, 128),
                    xnT[:, ic * TQ + th2 * 512: ic * TQ + (th2 + 1) * 512],
                    start=(ic == 0), stop=(ic == NIC - 1)),
                    reads=(f"wcc{h % 2}", "xnT"), writes=(f"PB{th2}",))

    for h in range(NH):
        cc = head_weight(wkT_h, h)
        head_proj(cc, h)
        kev = KEV[h % 2]
        for th2 in range(2):
            em.op(V, lambda q, th2=th2, kev=kev, h=h: q.tensor_scalar(
                out=arb(kev + th2 * 512, 512), in0=PB[th2][:],
                scalar1=bks[:, h:h + 1], scalar2=None, op0=A.add),
                reads=(f"PB{th2}", "bks"), writes=(f"kev{h % 2}",))
        em.op(G, lambda q, h=h, kev=kev: q.dma_start(
            out=kcc_in.ap()[h * HD:(h + 1) * HD, :], in_=arb(kev, 1024)),
            reads=(f"kev{h % 2}",), writes=("kcc_in",), dma=True)
        em.op(C, lambda q, cc=cc: q.activation(
            out=dead[:], in_=arb(cc, 4096), func=AF.Square, accum_out=lamsq[:]),
            reads=(f"wcc{h % 2}",), writes=("lamsq",))
        em.op(V, lambda q: q.tensor_copy(lamb[:], lamsq[:]),
              reads=("lamsq",), writes=("lamb",))
        em.op(T, lambda q: q.matmul(PB[2][0:1, 0:1], lamb[:], onesc[:],
                                    start=True, stop=True),
              reads=("lamb", "onesc"), writes=("PB2",))
        em.op(V, lambda q, h=h: q.tensor_copy(SM[0:1, 64 + h:65 + h],
                                              PB[2][0:1, 0:1]),
              reads=("PB2",), writes=("lamrow",))

    for h in range(NH):
        cc = head_weight(wvT_h, h)
        head_proj(cc, h)
        kev = KEV[h % 2]
        for th2 in range(2):
            em.op(V, lambda q, th2=th2, kev=kev, h=h: q.tensor_scalar(
                out=arb(kev + th2 * 512, 512), in0=PB[th2][:],
                scalar1=bvs[:, h:h + 1], scalar2=None, op0=A.add),
                reads=(f"PB{th2}", "bvs"), writes=(f"kev{h % 2}",))
        for tt in range(NT):
            em.op(T, lambda q, kev=kev, tt=tt: q.transpose(
                PT[0][:, tt * 128:(tt + 1) * 128], arb(kev + tt * 128, 128), idn[:]),
                reads=(f"kev{h % 2}", "idn"), writes=("PT0",))
        em.op(V, lambda q: q.tensor_copy(arb(VTE, 1024), PT[0][:]),
              reads=("PT0",), writes=("vte",))
        em.op(G, lambda q, h=h: q.dma_start(
            out=vcc_in.ap()[h * TQ:(h + 1) * TQ, :].rearrange("(c p) d -> p c d", p=128),
            in_=arb(VTE, 1024).rearrange("p (c d) -> p c d", d=128)),
            reads=("vte",), writes=("vcc_in",), dma=True)

    em.op(G, lambda q: q.collective_compute(
        "AllGather", A.bypass, replica_groups=[[0, 1, 2, 3], [4, 5, 6, 7]],
        ins=[kcc_in.ap().opt()], outs=[kcc_out.ap().opt()]),
        reads=("kcc_in",), writes=("kcc_out",))
    em.op(G, lambda q: q.collective_compute(
        "AllGather", A.bypass, replica_groups=[[0, 1, 2, 3], [4, 5, 6, 7]],
        ins=[vcc_in.ap().opt()], outs=[vcc_out.ap().opt()]),
        reads=("vcc_in",), writes=("vcc_out",))

    em.barrier()

    QW = [0, 8192]
    DQ = [16384, 20480]
    LH = [24576, 28672]
    HB = [32768, 33792]
    SIL = 34816
    for jb in range(NJB):
        for mi, (q_hh, s_hh, z_hh) in enumerate(
                ((qg_h, sg_h, zg_h), (qu_h, su_h, zu_h))):
            sl = (jb * 2 + mi) % 2
            em.op(Y, lambda q, q_hh=q_hh, jb=jb, sl=sl: q.dma_start(
                out=ari32(QW[sl], 4096), in_=q_hh[jb * 128:(jb + 1) * 128, :]),
                writes=(f"qw{sl}",), dma=True)
            em.op(Y, lambda q, z_hh=z_hh, jb=jb, mi=mi: q.dma_start(
                out=ZS[:, mi * 64:mi * 64 + 32], in_=z_hh[jb * 128:(jb + 1) * 128, :]),
                writes=(f"zs{mi}",), dma=True)
            em.op(Y, lambda q, s_hh=s_hh, jb=jb, mi=mi: q.dma_start(
                out=ZS[:, mi * 64 + 32:mi * 64 + 64],
                in_=s_hh[jb * 128:(jb + 1) * 128, :]),
                writes=(f"zs{mi}",), dma=True)
            for ic in range(NIC):
                em.op(V, lambda q, sl=sl, ic=ic, mi=mi: q.tensor_scalar(
                    out=arb(DQ[sl] + ic * 128, 128),
                    in0=ari32(QW[sl] + ic * 256, 128),
                    scalar1=ZS[:, mi * 64 + ic:mi * 64 + ic + 1],
                    scalar2=ZS[:, mi * 64 + 32 + ic:mi * 64 + 33 + ic],
                    op0=A.subtract, op1=A.mult),
                    reads=(f"qw{sl}", f"zs{mi}"), writes=(f"dq{sl}",))
            for bk_i in range(4):
                pt = bk_i % 2
                for c in range(8):
                    ic = bk_i * 8 + c
                    em.op(T, lambda q, sl=sl, ic=ic, c=c, pt=pt: q.transpose(
                        PT[pt][:, c * 128:(c + 1) * 128],
                        arb(DQ[sl] + ic * 128, 128), idn[:]),
                        reads=(f"dq{sl}", "idn"), writes=(f"PT{pt}",))
                em.op(V, lambda q, sl=sl, pt=pt, bk_i=bk_i: q.tensor_copy(
                    arb(LH[sl] + bk_i * 1024, 1024), PT[pt][:]),
                    reads=(f"PT{pt}",), writes=(f"lh{sl}",))
            for th2 in range(2):
                pb = mi * 2 + th2
                for ic in range(NIC):
                    em.op(T, lambda q, sl=sl, ic=ic, th2=th2, pb=pb: q.matmul(
                        PB[pb][:], arb(LH[sl] + ic * 128, 128),
                        xnT[:, ic * TQ + th2 * 512: ic * TQ + (th2 + 1) * 512],
                        start=(ic == 0), stop=(ic == NIC - 1)),
                        reads=(f"lh{sl}", "xnT"), writes=(f"PB{pb}",))
        hb = HB[jb % 2]
        for th2 in range(2):
            gp, up = th2, 2 + th2
            em.op(C, lambda q, up=up: q.activation(
                out=arb(SIL, 512), in_=PB[up][:], func=AF.Identity),
                reads=(f"PB{up}",), writes=("ubuf",))
            em.op(C, lambda q, gp=gp: q.activation(
                out=arf32(SIL + 512, 512), in_=PB[gp][:], func=AF.Exp, scale=-1.0),
                reads=(f"PB{gp}",), writes=("eneg",))
            em.op(V, lambda q: q.tensor_scalar(
                out=arf32(SIL + 1536, 512), in0=arf32(SIL + 512, 512),
                scalar1=1.0, scalar2=None, op0=A.add),
                reads=("eneg",), writes=("den",))
            em.op(V, lambda q: q.reciprocal(
                out=arf32(SIL + 2560, 512), in_=arf32(SIL + 1536, 512)),
                reads=("den",), writes=("rec",))
            em.op(V, lambda q, gp=gp: q.tensor_tensor(
                out=arf32(SIL + 3584, 512), in0=PB[gp][:], in1=arb(SIL, 512),
                op=A.mult),
                reads=(f"PB{gp}", "ubuf"), writes=("gu",))
            em.op(V, lambda q, hb=hb, th2=th2: q.tensor_tensor(
                out=arb(hb + th2 * 512, 512), in0=arf32(SIL + 3584, 512),
                in1=arf32(SIL + 2560, 512), op=A.mult),
                reads=("gu", "rec"), writes=(f"hb{jb % 2}",))
        em.op(G, lambda q, jb=jb, hb=hb: q.dma_start(
            out=hT_d.ap()[jb * 128:(jb + 1) * 128, :], in_=arb(hb, 1024)),
            reads=(f"hb{jb % 2}",), writes=("hT_d",), dma=True)

    em.barrier()

    AK = [0, 4096]
    AV = [8192, 12288]
    AQ = [16384, 17408]
    APT = 18432
    AW = [26624, 30720, 34816, 38912]
    AWM = 43008
    AM = 47104
    PRW = 51200
    R_Q2 = PRW
    R_MU = PRW + 1024
    R_A = PRW + 2048
    R_B = PRW + 3072

    def kv_dma(h):
        ks, vs = f"ak{h % 2}", f"av{h % 2}"
        for r in range(4):
            em.op(Y, lambda q, h=h, r=r: q.dma_start(
                out=arb(AK[h % 2] + r * 1024, 1024),
                in_=kcc_out.ap()[r * NH * HD + h * HD: r * NH * HD + (h + 1) * HD, :]),
                reads=("kcc_out",), writes=(ks,), dma=True)
            em.op(Y, lambda q, h=h, r=r: q.dma_start(
                out=arb(AV[h % 2] + r * 1024, 1024).rearrange("p (c d) -> p c d", d=128),
                in_=vcc_out.ap()[r * NH * TQ + h * TQ: r * NH * TQ + (h + 1) * TQ,
                                 :].rearrange("(c p) d -> p c d", p=128)),
                reads=("vcc_out",), writes=(vs,), dma=True)

    SB = 128
    ST = {"n": 0, "n2": 2, "thr": 4, "nthr": 6, "cnt": 8, "uu": 10, "hcnt": 12,
          "nhi": 14, "thrhi": 16, "dmo": 18, "dm1": 20, "wst": 22, "Z": 24,
          "rz": 26, "csig": 28}

    def sc(ps, name, w=2):
        c0 = SB + ps * 32 + ST[name]
        return SM[:, c0:c0 + w]

    def d_head(h):
        ks, vs = f"ak{h % 2}", f"av{h % 2}"
        if h + 1 < NH:
            kv_dma(h + 1)
        aq = AQ[h % 2]
        aqk = f"aq{h % 2}"
        for ic in range(NIC):
            csl = ic % 2
            em.op(Y, lambda q, h=h, ic=ic, csl=csl: q.dma_start(
                out=qwc[:, csl * 128:(csl + 1) * 128],
                in_=wqT_h[ic * 128:(ic + 1) * 128, h * HD:(h + 1) * HD]),
                writes=(f"qwc{csl}",), dma=True)
            em.op(V, lambda q, csl=csl: q.tensor_copy(
                qwb[:, csl * 128:(csl + 1) * 128], qwc[:, csl * 128:(csl + 1) * 128]),
                reads=(f"qwc{csl}",), writes=(f"qwb{csl}",))
            for th2 in range(2):
                em.op(T, lambda q, csl=csl, ic=ic, th2=th2: q.matmul(
                    PB[th2][:], qwb[:, csl * 128:(csl + 1) * 128],
                    xnT[:, ic * TQ + th2 * 512: ic * TQ + (th2 + 1) * 512],
                    start=(ic == 0), stop=(ic == NIC - 1)),
                    reads=(f"qwb{csl}", "xnT"), writes=(f"PB{th2}",))
        for th2 in range(2):
            em.op(V, lambda q, th2=th2, aq=aq, h=h: q.tensor_scalar(
                out=arb(aq + th2 * 512, 512), in0=PB[th2][:],
                scalar1=bqs[:, h:h + 1], scalar2=None, op0=A.add),
                reads=(f"PB{th2}", "bqs"), writes=(aqk,))
        em.op(V, lambda q, h=h: q.tensor_scalar(
            out=dead[:], in0=arb(AK[h % 2], 4096), scalar1=1.0, scalar2=None,
            op0=A.mult, op1=A.add, accum_out=SM[:, 16:17]),
            reads=(ks,), writes=("kbar",))
        em.op(V, lambda q: q.tensor_scalar(
            out=kb16[:], in0=SM[:, 16:17], scalar1=float(1.0 / SEQ), scalar2=None,
            op0=A.mult), reads=("kbar",), writes=("kb16",))
        em.op(V, lambda q, aq=aq: q.tensor_tensor(
            out=sqb[:], in0=arb(aq, 1024), in1=arb(aq, 1024), op=A.mult),
            reads=(aqk,), writes=("sqb",))
        for th2 in range(2):
            em.op(T, lambda q, th2=th2: q.matmul(
                PB[2][0:1, :], onesc[:], sqb[:, th2 * 512:(th2 + 1) * 512],
                start=True, stop=True), reads=("sqb", "onesc"), writes=("PB2",))
            em.op(V, lambda q: q.tensor_copy(arf32(R_Q2, 512)[0:1, :], PB[2][0:1, :]),
                  reads=("PB2",), writes=("r_q2",))
            em.op(T, lambda q, th2=th2, aq=aq: q.matmul(
                PB[3][0:1, :], kb16[:], arb(aq + th2 * 512, 512),
                start=True, stop=True), reads=("kb16", aqk), writes=("PB3",))
            em.op(V, lambda q: q.tensor_copy(arf32(R_MU, 512)[0:1, :], PB[3][0:1, :]),
                  reads=("PB3",), writes=("r_mu",))
            em.op(V, lambda q, h=h: q.tensor_scalar(
                out=arf32(R_Q2, 512)[0:1, :], in0=arf32(R_Q2, 512)[0:1, :],
                scalar1=SM[0:1, 64 + h:65 + h], scalar2=float(1.0 / (128.0 * 128.0)),
                op0=A.mult, op1=A.mult),
                reads=("r_q2", "lamrow"), writes=("r_q2",))
            em.op(C, lambda q: q.activation(out=arf32(R_Q2, 512)[0:1, :],
                                            in_=arf32(R_Q2, 512)[0:1, :], func=AF.Ln),
                  reads=("r_q2",), writes=("r_q2",))
            em.op(C, lambda q: q.activation(out=arf32(R_A, 512)[0:1, :],
                                            in_=arf32(R_Q2, 512)[0:1, :], func=AF.Exp,
                                            scale=-0.5),
                  reads=("r_q2",), writes=("r_a",))
            em.op(C, lambda q: q.activation(out=arf32(R_B, 512)[0:1, :],
                                            in_=arf32(R_Q2, 512)[0:1, :], func=AF.Exp,
                                            scale=0.5),
                  reads=("r_q2",), writes=("r_sig",))
            em.op(V, lambda q: q.tensor_scalar(
                out=arf32(R_A, 512)[0:1, :], in0=arf32(R_A, 512)[0:1, :],
                scalar1=RS128, scalar2=None, op0=A.mult),
                reads=("r_a",), writes=("r_a",))
            em.op(V, lambda q: q.tensor_tensor(
                out=arf32(R_Q2, 512)[0:1, :], in0=arf32(R_MU, 512)[0:1, :],
                in1=arf32(R_A, 512)[0:1, :], op=A.mult),
                reads=("r_mu", "r_a"), writes=("r_q2",))
            em.op(V, lambda q: q.tensor_scalar(
                out=arf32(R_Q2, 512)[0:1, :], in0=arf32(R_Q2, 512)[0:1, :],
                scalar1=-1.0, scalar2=None, op0=A.mult),
                reads=("r_q2",), writes=("r_q2",))
            for qq in range(4):
                qt = th2 * 4 + qq
                for pi, (rbase, rkey) in enumerate(((R_A, "r_a"), (R_Q2, "r_q2"),
                                                    (R_B, "r_sig"))):
                    em.op(T, lambda q, rbase=rbase, qq=qq: q.transpose(
                        PB[2][0:128, 0:1], arf32(rbase + qq * 256, 128)[0:1, :],
                        idf[0:1, 0:1]), reads=(rkey, "idf"), writes=("PB2",))
                    em.op(V, lambda q, qt=qt, pi=pi: q.tensor_copy(
                        prm[:, qt * 3 + pi:qt * 3 + pi + 1], PB[2][0:128, 0:1]),
                        reads=("PB2",), writes=("prm",))
        for p in range(4):
            ps = p % 2
            ta, tb = 2 * p, 2 * p + 1
            for t_i, qt in enumerate((ta, tb)):
                ws = AW[qt % 4]
                wk_ = f"aw{qt % 4}"
                em_scores(h, qt, ws, wk_, aq, aqk, ks)
            sel_pair(h, ps, ta, tb)
            for t_i, qt in enumerate((ta, tb)):
                sel_finish(h, ps, t_i, qt)
            ctx_pair(h, p, ta, tb, vs)

    def em_scores(h, qt, ws, wk_, aq, aqk, ks):
        for c in range(8):
            pbn = c % 2
            em.op(T, lambda q, qt=qt, c=c, pbn=pbn, aq=aq, h=h: q.matmul(
                PB[pbn][:], arb(aq + qt * 128, 128),
                arb(AK[h % 2] + c * 512, 512), start=True, stop=True),
                reads=(aqk, ks), writes=(f"PB{pbn}",))
            em.op(C, lambda q, ws=ws, c=c, pbn=pbn, qt=qt: q.activation(
                out=arf16(ws + c * 512, 512), in_=PB[pbn][:], func=AF.Identity,
                scale=prm[:, qt * 3:qt * 3 + 1], bias=prm[:, qt * 3 + 1:qt * 3 + 2]),
                reads=(f"PB{pbn}", "prm"), writes=(wk_,))

    def sel_pair(h, ps, ta, tb):
        wka, wkb = f"aw{ta % 4}", f"aw{tb % 4}"
        wsa, wsb = AW[ta % 4], AW[tb % 4]
        em.op(V, lambda q, ps=ps: q.memset(sc(ps, "n"), float(2 ** (JBIS - 1))),
              writes=(f"sel_n{ps}",))
        em.op(V, lambda q, ps=ps: q.memset(sc(ps, "hcnt"), 0.0),
              writes=(f"sel_h{ps}",))
        cur, nxt = "n", "n2"
        for j in range(1, JBIS + 1):
            em.op(V, lambda q, ps=ps, cur=cur: q.tensor_scalar(
                out=sc(ps, "thr"), in0=sc(ps, cur), scalar1=DBIS, scalar2=WLO,
                op0=A.mult, op1=A.add),
                reads=(f"sel_n{ps}",), writes=(f"sel_t{ps}",))
            eng = CENG[j - 1]
            if eng == "se":
                em.op(V, lambda q, ps=ps: q.tensor_scalar(
                    out=sc(ps, "nthr"), in0=sc(ps, "thr"), scalar1=-1.0,
                    scalar2=None, op0=A.mult),
                    reads=(f"sel_t{ps}",), writes=(f"sel_nt{ps}",))
                for t_i, (ws, wk_) in enumerate(((wsa, wka), (wsb, wkb))):
                    em.op(C, lambda q, ws=ws, ps=ps, t_i=t_i: q.activation(
                        out=dead[:], in_=arf16(ws, 4096), func=AF.Sign,
                        bias=sc(ps, "nthr")[:, t_i:t_i + 1], scale=1.0,
                        accum_out=sc(ps, "csig")[:, t_i:t_i + 1]),
                        reads=(wk_, f"sel_nt{ps}"), writes=(f"sel_cs{ps}_{t_i}",)),
                em.op(V, lambda q, ps=ps: q.tensor_scalar(
                    out=sc(ps, "cnt"), in0=sc(ps, "csig"), scalar1=0.5,
                    scalar2=float(SEQ / 2.0), op0=A.mult, op1=A.add),
                    reads=(f"sel_cs{ps}_0", f"sel_cs{ps}_1"),
                    writes=(f"sel_c{ps}",))
            else:
                for t_i, (ws, wk_) in enumerate(((wsa, wka), (wsb, wkb))):
                    em.op(V, lambda q, ws=ws, ps=ps, t_i=t_i: q.tensor_scalar(
                        out=dead[:], in0=arf16(ws, 4096),
                        scalar1=sc(ps, "thr")[:, t_i:t_i + 1], scalar2=None,
                        op0=A.is_gt, op1=A.add,
                        accum_out=sc(ps, "cnt")[:, t_i:t_i + 1]),
                        reads=(wk_, f"sel_t{ps}"), writes=(f"sel_c{ps}",))
            em.op(V, lambda q, ps=ps: q.tensor_scalar(
                out=sc(ps, "uu"), in0=sc(ps, "cnt"), scalar1=float(KTOP),
                scalar2=0.5, op0=A.is_ge, op1=A.subtract),
                reads=(f"sel_c{ps}",), writes=(f"sel_u{ps}",))
            em.op(V, lambda q, ps=ps: q.tensor_scalar(
                out=mlt[:], in0=sc(ps, "cnt"), scalar1=float(KTOP), scalar2=None,
                op0=A.is_lt), reads=(f"sel_c{ps}",), writes=(f"sel_m{ps}",))
            em.op(V, lambda q, ps=ps: q.copy_predicated(
                sc(ps, "hcnt"), mlt[:], sc(ps, "cnt")),
                reads=(f"sel_m{ps}", f"sel_c{ps}"), writes=(f"sel_h{ps}",))
            if j < JBIS:
                em.op(V, lambda q, ps=ps, cur=cur, nxt=nxt, j=j: q.scalar_tensor_tensor(
                    out=sc(ps, nxt), in0=sc(ps, "uu"),
                    scalar=float(2 ** (JBIS - j)), in1=sc(ps, cur),
                    op0=A.mult, op1=A.add),
                    reads=(f"sel_u{ps}", f"sel_n{ps}"), writes=(f"sel_n{ps}",))
                cur, nxt = nxt, cur
            else:
                em.op(V, lambda q, ps=ps, cur=cur: q.scalar_tensor_tensor(
                    out=sc(ps, "nhi"), in0=sc(ps, "uu"), scalar=0.5,
                    in1=sc(ps, cur), op0=A.add, op1=A.add),
                    reads=(f"sel_u{ps}", f"sel_n{ps}"), writes=(f"sel_nh{ps}",))
        em.op(V, lambda q, ps=ps: q.tensor_scalar(
            out=sc(ps, "thrhi"), in0=sc(ps, "nhi"), scalar1=DBIS, scalar2=WLO,
            op0=A.mult, op1=A.add), reads=(f"sel_nh{ps}",), writes=(f"sel_th{ps}",))
        em.op(V, lambda q, ps=ps: q.tensor_scalar(
            out=sc(ps, "dmo"), in0=sc(ps, "hcnt"), scalar1=-1.0,
            scalar2=float(KTOP - 1) + 0.75, op0=A.mult, op1=A.add),
            reads=(f"sel_h{ps}",), writes=(f"sel_d{ps}",))
        em.op(V, lambda q, ps=ps: q.tensor_scalar(
            out=sc(ps, "dm1"), in0=sc(ps, "dmo"), scalar1=1.0, scalar2=None,
            op0=A.subtract), reads=(f"sel_d{ps}",), writes=(f"sel_d1{ps}",))

    def sel_finish(h, ps, t_i, qt):
        ws = AW[qt % 4]
        wk_ = f"aw{qt % 4}"
        em.op(V, lambda q, ws=ws, ps=ps, t_i=t_i: q.tensor_scalar(
            out=arb(AWM, 4096), in0=arf16(ws, 4096),
            scalar1=sc(ps, "thrhi")[:, t_i:t_i + 1], scalar2=None, op0=A.is_le),
            reads=(wk_, f"sel_th{ps}"), writes=("awm",))
        em.op(G, lambda q, ws=ws: q.tensor_tensor(
            out=arf16(AM, 4096), in0=arb(AWM, 4096), in1=arf16(ws, 4096),
            op=A.mult), reads=("awm", wk_), writes=("am",))
        em.op(V, lambda q, t_i=t_i: q.max(out=m8[:, t_i * 8:(t_i + 1) * 8],
                                          in_=arf16(AM, 4096)),
              reads=("am",), writes=(f"m8_{t_i}",))
        em.op(V, lambda q, ps=ps, t_i=t_i: q.tensor_scalar(
            out=t8[:, t_i * 8:(t_i + 1) * 8], in0=io8[:],
            scalar1=sc(ps, "dmo")[:, t_i:t_i + 1], scalar2=None, op0=A.is_le),
            reads=(f"sel_d{ps}", "io8"), writes=(f"t8_{t_i}",))
        em.op(V, lambda q, ps=ps, t_i=t_i: q.scalar_tensor_tensor(
            out=j8[:, t_i * 8:(t_i + 1) * 8], in0=io8[:],
            scalar=sc(ps, "dm1")[:, t_i:t_i + 1],
            in1=t8[:, t_i * 8:(t_i + 1) * 8], op0=A.is_gt, op1=A.mult),
            reads=(f"sel_d1{ps}", "io8", f"t8_{t_i}"), writes=(f"j8_{t_i}",))
        em.op(V, lambda q, ps=ps, t_i=t_i: q.scalar_tensor_tensor(
            out=s8[:, t_i * 8:(t_i + 1) * 8],
            in0=m8[:, t_i * 8:(t_i + 1) * 8], scalar=1.0,
            in1=j8[:, t_i * 8:(t_i + 1) * 8], op0=A.mult, op1=A.mult,
            accum_out=sc(ps, "wst")[:, t_i:t_i + 1]),
            reads=(f"m8_{t_i}", f"j8_{t_i}"), writes=(f"sel_w{ps}",))
        em.op(V, lambda q, ws=ws, ps=ps, t_i=t_i: q.tensor_scalar(
            out=arb(AWM, 4096), in0=arf16(ws, 4096),
            scalar1=sc(ps, "wst")[:, t_i:t_i + 1], scalar2=NEGBIG,
            op0=A.is_lt, op1=A.mult),
            reads=(wk_, f"sel_w{ps}"), writes=("awm",))
        em.op(G, lambda q, ws=ws: q.tensor_tensor(
            out=arf16(AM, 4096), in0=arf16(ws, 4096), in1=arb(AWM, 4096),
            op=A.add), reads=("awm", wk_), writes=("am",))
        em.op(C, lambda q, ps=ps, t_i=t_i, qt=qt: q.activation(
            out=arb(AWM, 4096), in_=arf16(AM, 4096), func=AF.Exp,
            scale=prm[:, qt * 3 + 2:qt * 3 + 3],
            accum_out=sc(ps, "Z")[:, t_i:t_i + 1]),
            reads=("am", "prm"), writes=("awm", f"sel_z{ps}_{t_i}",)),
        em.op(V, lambda q, ps=ps, t_i=t_i: q.reciprocal(
            out=sc(ps, "rz")[:, t_i:t_i + 1], in_=sc(ps, "Z")[:, t_i:t_i + 1]),
            reads=(f"sel_z{ps}_{t_i}",), writes=(f"sel_r{ps}_{t_i}",))
        em.op(V, lambda q, ps=ps, t_i=t_i: q.tensor_scalar(
            out=arb(AM, 4096), in0=arb(AWM, 4096),
            scalar1=sc(ps, "rz")[:, t_i:t_i + 1], scalar2=None, op0=A.mult),
            reads=("awm", f"sel_r{ps}_{t_i}"), writes=("am",))
        for bk_i in range(4):
            for c in range(8):
                kb = bk_i * 8 + c
                em.op(T, lambda q, kb=kb, c=c: q.transpose(
                    PT[0][:, c * 128:(c + 1) * 128], arb(AM + kb * 128, 128),
                    idn[:]), reads=("am", "idn"), writes=("PT0",))
            em.op(V, lambda q, bk_i=bk_i, t_i=t_i: q.tensor_copy(
                arb(APT, 8192).rearrange("p (k u) -> p k u", u=256)[
                    :, bk_i * 8:(bk_i + 1) * 8, t_i * 128:(t_i + 1) * 128],
                PT[0][:].rearrange("p (c u) -> p c u", u=128)),
                reads=("PT0",), writes=("apt",))

    def ctx_pair(h, p, ta, tb, vs):
        for kb in range(NIC):
            em.op(T, lambda q, kb=kb, h=h: q.matmul(
                PB[2][0:128, 0:256], arb(AV[h % 2] + kb * 128, 128),
                arb(APT + kb * 256, 256), start=(kb == 0), stop=(kb == NIC - 1)),
                reads=(vs, "apt"), writes=("PB2",))
        em.op(V, lambda q: q.tensor_copy(ctst[:], PB[2][0:128, 0:256]),
              reads=("PB2",), writes=("ctst",))
        em.op(G, lambda q, h=h, p=p: q.dma_start(
            out=ctx_d.ap()[h * HD:(h + 1) * HD, p * 256:(p + 1) * 256],
            in_=ctst[:]), reads=("ctst",), writes=("ctx_d",), dma=True)

    def c2_unit(u):
        th = u // 16
        oc0 = (u % 16) * 2
        if th == 0:
            for o2 in range(2):
                em.op(Y, lambda q, oc0=oc0, o2=o2: q.dma_start(
                    out=ZSD[:, (o2 * 2) * NJB:(o2 * 2 + 1) * NJB],
                    in_=zd_h[(oc0 + o2) * 128:(oc0 + o2 + 1) * 128, :]),
                    writes=(f"zsd{o2}",), dma=True)
                em.op(Y, lambda q, oc0=oc0, o2=o2: q.dma_start(
                    out=ZSD[:, (o2 * 2 + 1) * NJB:(o2 * 2 + 2) * NJB],
                    in_=sd_h[(oc0 + o2) * 128:(oc0 + o2 + 1) * 128, :]),
                    writes=(f"zsd{o2}",), dma=True)
        for jb in range(NJB):
            js = jb % 2
            em.op(Y, lambda q, jb=jb, th=th, js=js: q.dma_start(
                out=dht[:, js * 512:(js + 1) * 512],
                in_=hT_d.ap()[jb * 128:(jb + 1) * 128, th * 512:(th + 1) * 512]),
                reads=("hT_d",), writes=(f"dht{js}",), dma=True)
            for o2 in range(2):
                oc = oc0 + o2
                tile_row = (jb * NIC + oc) * 128
                if th == 0:
                    em.op(Y, lambda q, oc=oc, jb=jb, o2=o2: q.dma_start(
                        out=dqc[:, o2 * 128:(o2 + 1) * 128],
                        in_=qd_h[oc * 128:(oc + 1) * 128, jb * 128:(jb + 1) * 128]),
                        writes=(f"dqc{o2}",), dma=True)
                    em.op(V, lambda q, o2=o2, jb=jb: q.tensor_scalar(
                        out=dqb[:, o2 * 128:(o2 + 1) * 128],
                        in0=dqc[:, o2 * 128:(o2 + 1) * 128],
                        scalar1=ZSD[:, (o2 * 2) * NJB + jb:(o2 * 2) * NJB + jb + 1],
                        scalar2=ZSD[:, (o2 * 2 + 1) * NJB + jb:(o2 * 2 + 1) * NJB + jb + 1],
                        op0=A.subtract, op1=A.mult),
                        reads=(f"dqc{o2}", f"zsd{o2}"), writes=(f"dqb{o2}",))
                    em.op(T, lambda q, o2=o2: q.transpose(
                        PT[1][:, o2 * 128:(o2 + 1) * 128],
                        dqb[:, o2 * 128:(o2 + 1) * 128], idn[:]),
                        reads=(f"dqb{o2}", "idn"), writes=("PT1",))
                    em.op(V, lambda q, o2=o2: q.tensor_copy(
                        dlh[:, o2 * 128:(o2 + 1) * 128],
                        PT[1][:, o2 * 128:(o2 + 1) * 128]),
                        reads=("PT1",), writes=(f"dlh{o2}",))
                    em.op(G, lambda q, tile_row=tile_row, o2=o2: q.dma_start(
                        out=wdt_d.ap()[tile_row:tile_row + 128, :],
                        in_=dlh[:, o2 * 128:(o2 + 1) * 128]),
                        reads=(f"dlh{o2}",), writes=("wdt_d",), dma=True)
                else:
                    em.op(Y, lambda q, tile_row=tile_row, o2=o2: q.dma_start(
                        out=dlh[:, o2 * 128:(o2 + 1) * 128],
                        in_=wdt_d.ap()[tile_row:tile_row + 128, :]),
                        reads=("wdt_d",), writes=(f"dlh{o2}",), dma=True)
                em.op(T, lambda q, o2=o2, jb=jb, js=js: q.matmul(
                    PB[4 + o2][:], dlh[:, o2 * 128:(o2 + 1) * 128],
                    dht[:, js * 512:(js + 1) * 512],
                    start=(jb == 0), stop=(jb == NJB - 1)),
                    reads=(f"dlh{o2}", f"dht{js}"), writes=(f"PB{4 + o2}",))
        for o2 in range(2):
            em.op(V, lambda q, o2=o2: q.tensor_copy(
                dos[:, o2 * 512:(o2 + 1) * 512], PB[4 + o2][:]),
                reads=(f"PB{4 + o2}",), writes=(f"dos{o2}",))
            em.op(G, lambda q, oc0=oc0, o2=o2, th=th: q.dma_start(
                out=ffn_d.ap()[(oc0 + o2) * 128:(oc0 + o2 + 1) * 128,
                               th * 512:(th + 1) * 512],
                in_=dos[:, o2 * 512:(o2 + 1) * 512]),
                reads=(f"dos{o2}",), writes=("ffn_d",), dma=True)

    kv_dma(0)
    for h in range(NH):
        d_head(h)
        c2_unit(h)

    em.barrier()

    ECX = [0, 2048]
    EWF = 4096
    EWB = 6144
    EOS = 8192
    EXS = [12288, 14336]
    EOT = [16384, 18432]
    for ocg in range(16):
        for dc in range(NIC):
            dsl = dc % 2
            em.op(Y, lambda q, dc=dc, dsl=dsl: q.dma_start(
                out=arb(ECX[dsl], 1024),
                in_=ctx_d.ap()[dc * 128:(dc + 1) * 128, :]),
                reads=("ctx_d",), writes=(f"ecx{dsl}",), dma=True)
            for o2 in range(2):
                oc = ocg * 2 + o2
                em.op(Y, lambda q, dc=dc, oc=oc, o2=o2, dsl=dsl: q.dma_start(
                    out=arf32(EWF + (dsl * 2 + o2) * 256, 128),
                    in_=woT_h[dc * 128:(dc + 1) * 128, oc * 128:(oc + 1) * 128]),
                    writes=(f"ewf{dsl}{o2}",), dma=True)
                em.op(V, lambda q, o2=o2, dsl=dsl: q.tensor_copy(
                    arb(EWB + (dsl * 2 + o2) * 128, 128),
                    arf32(EWF + (dsl * 2 + o2) * 256, 128)),
                    reads=(f"ewf{dsl}{o2}",), writes=(f"ewb{dsl}{o2}",))
                for half in range(2):
                    pbn = o2 * 2 + half
                    em.op(T, lambda q, o2=o2, half=half, pbn=pbn, dc=dc, dsl=dsl: q.matmul(
                        PB[pbn][:], arb(EWB + (dsl * 2 + o2) * 128, 128),
                        arb(ECX[dsl] + half * 512, 512),
                        start=(dc == 0), stop=(dc == NIC - 1)),
                        reads=(f"ewb{dsl}{o2}", f"ecx{dsl}"), writes=(f"PB{pbn}",))
        for o2 in range(2):
            oc = ocg * 2 + o2
            for half in range(2):
                pbn = o2 * 2 + half
                em.op(Y, lambda q, oc=oc, half=half: q.dma_start(
                    out=arf32(EXS[half], 512),
                    in_=ffn_d.ap()[oc * 128:(oc + 1) * 128,
                                   half * 512:(half + 1) * 512]),
                    reads=("ffn_d",), writes=(f"exf{half}",), dma=True)
                em.op(V, lambda q, pbn=pbn, half=half, oc=oc: q.scalar_tensor_tensor(
                    out=arf32(EOS + (pbn % 2) * 1024, 512), in0=PB[pbn][:],
                    scalar=bosc[:, oc:oc + 1],
                    in1=arf32(EXS[half], 512), op0=A.add, op1=A.add),
                    reads=(f"PB{pbn}", f"exf{half}", "bosc"),
                    writes=(f"eos{pbn % 2}",))
                for tq4 in range(4):
                    em.op(T, lambda q, pbn=pbn, tq4=tq4: q.transpose(
                        PB[4][:, tq4 * 128:(tq4 + 1) * 128],
                        arf32(EOS + (pbn % 2) * 1024 + tq4 * 256, 128), idf[:]),
                        reads=(f"eos{pbn % 2}", "idf"), writes=("PB4",))
                em.op(Y, lambda q, oc=oc, half=half: q.dma_start(
                    out=arf32(EXS[half], 512).rearrange("p (c u) -> p c u", u=128),
                    in_=x_h[half * 512:(half + 1) * 512,
                            oc * 128:(oc + 1) * 128].rearrange(
                        "(c p) u -> p c u", p=128)),
                    writes=(f"exf{half}",), dma=True)
                em.op(V, lambda q, half=half: q.tensor_tensor(
                    out=arf32(EOT[0], 512), in0=PB[4][:],
                    in1=arf32(EXS[half], 512), op=A.add),
                    reads=("PB4", f"exf{half}"), writes=("eot",))
                em.op(G, lambda q, oc=oc, half=half: q.dma_start(
                    out=out_h[half * 512:(half + 1) * 512,
                              oc * 128:(oc + 1) * 128].rearrange(
                        "(c p) u -> p c u", p=128),
                    in_=arf32(EOT[0], 512).rearrange("p (c u) -> p c u", u=128)),
                    reads=("eot",), writes=("out",), dma=True)

    em.barrier()

    with nc.Block() as block:
        @block.sync
        def _(q):
            for fn in em.plan["sync"]:
                fn(q)

        @block.vector
        def _(q):
            for fn in em.plan["vector"]:
                fn(q)

        @block.scalar
        def _(q):
            for fn in em.plan["scalar"]:
                fn(q)

        @block.tensor
        def _(q):
            for fn in em.plan["tensor"]:
                fn(q)

        @block.gpsimd
        def _(q):
            for fn in em.plan["gpsimd"]:
                fn(q)

    es.close()
    return nc


_CACHED = {}


def kernel(**inputs):
    x = np.ascontiguousarray(np.asarray(inputs["L0_attn_hidden_states"], dtype=np.float32))
    wq = np.asarray(inputs["L0_attn_wq"], dtype=np.float32)
    wk = np.asarray(inputs["L0_attn_wk"], dtype=np.float32)
    wv = np.asarray(inputs["L0_attn_wv"], dtype=np.float32)
    wo = np.asarray(inputs["L0_attn_wo"], dtype=np.float32)
    shared = {
        "wqT": np.ascontiguousarray(wq.T),
        "wkT": np.ascontiguousarray(wk.T),
        "wvT": np.ascontiguousarray(wv.T),
        "woT": np.ascontiguousarray(wo.T),
        "bq": np.ascontiguousarray(np.asarray(inputs["L0_attn_bq"], np.float32).reshape(NH, HD).T),
        "bk": np.ascontiguousarray(np.asarray(inputs["L0_attn_bk"], np.float32).reshape(NH, HD).T),
        "bv": np.ascontiguousarray(np.asarray(inputs["L0_attn_bv"], np.float32).reshape(NH, HD).T),
        "bo": np.ascontiguousarray(np.asarray(inputs["L0_attn_bo"], np.float32).reshape(NIC, 128).T),
        "qg": np.ascontiguousarray(np.asarray(inputs["L0_ffn_qw_gate"], np.int32)),
        "qu": np.ascontiguousarray(np.asarray(inputs["L0_ffn_qw_up"], np.int32)),
        "qd": np.ascontiguousarray(np.asarray(inputs["L0_ffn_qw_down"], np.int32)),
        "sg": np.ascontiguousarray(np.asarray(inputs["L0_ffn_s_gate"], np.float32)),
        "zg": np.ascontiguousarray(np.asarray(inputs["L0_ffn_z_gate"], np.float32)),
        "su": np.ascontiguousarray(np.asarray(inputs["L0_ffn_s_up"], np.float32)),
        "zu": np.ascontiguousarray(np.asarray(inputs["L0_ffn_z_up"], np.float32)),
        "sd": np.ascontiguousarray(np.asarray(inputs["L0_ffn_s_down"], np.float32)),
        "zd": np.ascontiguousarray(np.asarray(inputs["L0_ffn_z_down"], np.float32)),
    }
    if "nc" not in _CACHED:
        _CACHED["nc"] = build_program()
    nc = _CACHED["nc"]
    in_maps = []
    for core in range(8):
        b, sl = core // 4, core % 4
        m = dict(shared)
        m["x"] = np.ascontiguousarray(x[b, sl * TQ:(sl + 1) * TQ, :])
        in_maps.append(m)
    res = run_bass_kernel_spmd(nc, in_maps, core_ids=list(range(8)))
    out = np.empty((2, SEQ, DM), np.float32)
    for core in range(8):
        b, sl = core // 4, core % 4
        out[b, sl * TQ:(sl + 1) * TQ, :] = res.results[core]["out"]
    return out
